# version 01
# baseline (speedup 1.0000x reference)
import contextlib
import numpy as np
import ml_dtypes
import concourse.bass as bass
import concourse.mybir as mybir
from concourse.bass_utils import run_bass_kernel_spmd

F32 = mybir.dt.float32
BF16 = mybir.dt.bfloat16
I32 = mybir.dt.int32
AF = mybir.ActivationFunctionType
ALU = mybir.AluOpType
AX = mybir.AxisListType

TOK = 2048
NT = 16
D = 1024
NF = 8
CAP = 4301.0
NCH = 8
EPS0 = 1e-5
EPS1 = 64e-5


class SemCtr:
    def __init__(self, h):
        self.h = h
        self.n = 0

    def inc(self, instr, k=1):
        instr.then_inc(self.h, k)
        self.n += k
        return self.n


class _InstRec:
    __slots__ = ("incs",)

    def __init__(self):
        self.incs = []

    def then_inc(self, sem, n=1):
        self.incs.append((sem, n))
        return self


class _EngRec:

    def __init__(self, name):
        self._name = name
        self._ops = []

    def __getattr__(self, meth):
        if meth.startswith("_"):
            raise AttributeError(meth)

        def call(*a, **k):
            inst = _InstRec()
            self._ops.append((meth, a, k, inst))
            return inst
        return call

    @staticmethod
    def _ap_names(a, k):
        names = []
        for x in list(a) + list(k.values()):
            nm = getattr(getattr(x, "tensor", None), "name", None)
            if nm is not None:
                names.append(nm)
        return names

    def _replay(self, eng):
        guard = self._name == "vector"
        prev = set()
        for meth, a, k, inst in self._ops:
            cur = set(self._ap_names(a, k)) if guard else ()
            if guard:
                if meth in ("wait_ge", "nop"):
                    cur = prev
                elif prev & cur:
                    eng.drain()
            r = getattr(eng, meth)(*a, **k)
            for sem, n in inst.incs:
                r.then_inc(sem, n)
            if guard and meth not in ("wait_ge", "nop"):
                prev = cur


def build(dbg=()):
    nc = bass.Bass("TRN2", target_bir_lowering=False)
    DT = nc.dram_tensor

    x_own = DT("x_own", [TOK, D], F32, kind="ExternalInput").ap()
    x_oth = DT("x_oth", [TOK, D], F32, kind="ExternalInput").ap()
    w_qkv = DT("w_qkv", [D, 3 * D], F32, kind="ExternalInput").ap()
    r_w = DT("r_w", [D, 16], F32, kind="ExternalInput").ap()
    rb_bc = DT("rb_bc", [128, 16], F32, kind="ExternalInput").ap()
    w1 = DT("w1", [16, D, 2048], F32, kind="ExternalInput").ap()
    b1c = DT("b1c", [128, 16, 16], F32, kind="ExternalInput").ap()
    w2 = DT("w2", [16, 2048, D], F32, kind="ExternalInput").ap()
    b2r = DT("b2r", [16, D], F32, kind="ExternalInput").ap()
    wrT = DT("wrT", [D, D], F32, kind="ExternalInput").ap()
    wkT = DT("wkT", [D, D], F32, kind="ExternalInput").ap()
    wvT = DT("wvT", [D, D], F32, kind="ExternalInput").ap()
    wgT = DT("wgT", [D, D], F32, kind="ExternalInput").ap()
    woT = DT("woT", [D, D], F32, kind="ExternalInput").ap()
    maac = DT("maac", [128, 8, 4], F32, kind="ExternalInput").ap()
    decay_in = DT("decay", [16, 1], F32, kind="ExternalInput").ap()
    faaaa_in = DT("faaaa", [16, 1], F32, kind="ExternalInput").ap()
    lng_bc_in = DT("lng_bc", [128, D], F32, kind="ExternalInput").ap()
    lnb_bc_in = DT("lnb_bc", [128, D], F32, kind="ExternalInput").ap()
    WgTin = DT("WgT", [D, 8], F32, kind="ExternalInput").ap()
    W1 = DT("W1", [8, D, 4096], F32, kind="ExternalInput").ap()
    fb1c = DT("fb1c", [128, 8, 32], F32, kind="ExternalInput").ap()
    W2 = DT("W2", [8, 4096, D], F32, kind="ExternalInput").ap()
    fb2r = DT("fb2r", [8, D], F32, kind="ExternalInput").ap()
    identb_in = DT("identb", [128, 128], BF16, kind="ExternalInput").ap()
    identf_in = DT("identf", [128, 128], F32, kind="ExternalInput").ap()
    ones1f_in = DT("ones1f", [1, 128], F32, kind="ExternalInput").ap()
    myhot_in = DT("myhot", [1, 8], F32, kind="ExternalInput").ap()
    maskc_in = DT("maskc", [8, 3], F32, kind="ExternalInput").ap()
    halfc_in = DT("halfc", [64, 2], F32, kind="ExternalInput").ap()
    epsc_in = DT("epsc", [128, 2], F32, kind="ExternalInput").ap()

    y_out = DT("y_own", [TOK, D], F32, kind="ExternalOutput").ap()
    taps = {}
    for name, shp, dt in [("tap_xn", [128, NF, TOK + 1], BF16),
                          ("tap_acc0", [128, NT, D], F32),
                          ("tap_c0", [128, NT, 16], F32),
                          ("tap_c1", [128, NT, 8], F32),
                          ("tap_y", [128, NT, D], BF16),
                          ("tap_acc1", [128, NT, D], F32),
                          ("tap_x1", [TOK, D], F32),
                          ("tap_xn1", [128, NF, TOK + 1], BF16),
                          ("tap_mk", [128, NT, 2], F32),
                          ("tap_gates", [128, NT, 8], F32)]:
        if name in dbg:
            taps[name] = DT(name, shp, dt, kind="ExternalOutput").ap()

    q_scr = DT("q_scr", [D, TOK], BF16, kind="Internal").ap()
    k_scr = DT("k_scr", [D, 2 * TOK], BF16, kind="Internal").ap()
    v_scr = DT("v_scr", [16, 2 * TOK, 65], BF16, kind="Internal").ap()
    x1_scr = DT("x1_scr", [TOK, D], F32, kind="Internal").ap()
    cc1_in = DT("cc1_in", [8, 1040], F32, kind="Internal")
    cc1_out = DT("cc1_out", [8, 1040], F32, kind="Internal")
    cc2_in = DT("cc2_in", [64, 1024], F32, kind="Internal")
    cc2_out = DT("cc2_out", [64, 1024], F32, kind="Internal")

    st = contextlib.ExitStack()
    with st:
        sX = SemCtr(st.enter_context(nc.semaphore("sX")))
        sL = SemCtr(st.enter_context(nc.semaphore("sL")))
        sT = SemCtr(st.enter_context(nc.semaphore("sT")))
        sS = SemCtr(st.enter_context(nc.semaphore("sS")))
        sV = SemCtr(st.enter_context(nc.semaphore("sV")))
        sG = SemCtr(st.enter_context(nc.semaphore("sG")))
        sO = SemCtr(st.enter_context(nc.semaphore("sO")))
        sB = SemCtr(st.enter_context(nc.semaphore("sB")))

        xn_sb = st.enter_context(nc.sbuf_tensor("xn_sb", [128, NF, TOK + 1], BF16))
        identb = st.enter_context(nc.sbuf_tensor("identb_s", [128, 128], BF16))
        identf = st.enter_context(nc.sbuf_tensor("identf_s", [128, 128], F32))
        ones1f = st.enter_context(nc.sbuf_tensor("ones1f_s", [1, 128], F32))
        myhot = st.enter_context(nc.sbuf_tensor("myhot_s", [1, 8], F32))
        maskc = st.enter_context(nc.sbuf_tensor("maskc_s", [8, 3], F32))
        halfc = st.enter_context(nc.sbuf_tensor("halfc_s", [64, 2], F32))
        onescol = st.enter_context(nc.sbuf_tensor("onescol", [128, 1], F32))
        c1 = st.enter_context(nc.sbuf_tensor("c1", [128, NT, 8], F32))
        c1T = st.enter_context(nc.sbuf_tensor("c1T", [8, TOK], BF16))
        stats = st.enter_context(nc.sbuf_tensor("stats", [128, 2, 8], F32))
        junk = st.enter_context(nc.sbuf_tensor("junk", [128, 1024], BF16))
        maasb = st.enter_context(nc.sbuf_tensor("maasb", [128, 8, 4], F32))
        b1sb = st.enter_context(nc.sbuf_tensor("b1sb", [128, 16, 16], F32))
        fb1sb = st.enter_context(nc.sbuf_tensor("fb1sb", [128, 8, 32], F32))
        epsc = st.enter_context(nc.sbuf_tensor("epsc_s", [128, 2], F32))

        block = st.enter_context(nc.Block())
        H = {"sp": _EngRec("sync"), "g": _EngRec("gpsimd"),
             "t": _EngRec("tensor"), "s": _EngRec("scalar"),
             "v": _EngRec("vector")}
        sp, g, t, s, v = H["sp"], H["g"], H["t"], H["s"], H["v"]
        ALL_ENG = (sp, g, t, s, v)

        def bar():
            tgt = sB.n + len(ALL_ENG)
            for e in ALL_ENG:
                sB.inc(e.nop(), 1)
            for e in ALL_ENG:
                e.wait_ge(sB.h, tgt)

        for dst, src in ((identb, identb_in), (identf, identf_in),
                         (ones1f, ones1f_in), (myhot, myhot_in),
                         (maskc, maskc_in), (halfc, halfc_in),
                         (maasb, maac), (b1sb, b1c), (fb1sb, fb1c),
                         (epsc, epsc_in)):
            sX.inc(sp.dma_start(out=dst[...], in_=src), 16)
        v.memset(onescol[:, :], 1.0)
        sV.inc(v.nop())
        bar()

        def emit_ln(xin, xout, sl, eps_ap):
            sS.inc(s.activation(junk[:, :], xin, AF.Square,
                                accum_out=stats[:, sl, 0:1]))
            s_ssq = sS.n
            sV.inc(v.tensor_reduce(stats[:, sl, 1:2], xin, AX.X, ALU.add))
            v.wait_ge(sS.h, s_ssq)
            sV.inc(v.tensor_scalar(stats[:, sl, 2:3], stats[:, sl, 1:2],
                                   1.0 / D, None, ALU.mult))
            sV.inc(v.tensor_scalar(stats[:, sl, 3:4], stats[:, sl, 0:1],
                                   1.0 / D, None, ALU.mult))
            sV.inc(v.tensor_tensor(stats[:, sl, 4:5], stats[:, sl, 2:3],
                                   stats[:, sl, 2:3], ALU.mult))
            sV.inc(v.tensor_tensor(stats[:, sl, 4:5], stats[:, sl, 3:4],
                                   stats[:, sl, 4:5], ALU.subtract))
            v_var = sV.n
            s.wait_ge(sV.h, v_var)
            sS.inc(s.activation(stats[:, sl, 5:6], stats[:, sl, 4:5], AF.Ln,
                                bias=eps_ap))
            sS.inc(s.activation(stats[:, sl, 6:7], stats[:, sl, 5:6], AF.Exp,
                                scale=-0.5))
            s_rstd = sS.n
            v.wait_ge(sS.h, s_rstd)
            sV.inc(v.scalar_tensor_tensor(stats[:, sl, 7:8], stats[:, sl, 2:3],
                                          -1.0, stats[:, sl, 6:7],
                                          ALU.mult, ALU.mult))
            v_b = sV.n
            s.wait_ge(sV.h, v_b)
            sS.inc(s.activation(xout, xin, AF.Identity,
                                scale=stats[:, sl, 6:7], bias=stats[:, sl, 7:8]))
            return sS.n

        stL0 = contextlib.ExitStack()
        with stL0:
            rbbc = stL0.enter_context(nc.sbuf_tensor("rbbc", [128, 16], F32))
            c0 = stL0.enter_context(nc.sbuf_tensor("c0", [128, NT, 16], F32))
            c0T = stL0.enter_context(nc.sbuf_tensor("c0T", [16, TOK], BF16))
            xa = stL0.enter_context(nc.sbuf_tensor("xa", [128, 2, D], F32))
            xnt = stL0.enter_context(nc.sbuf_tensor("xnt", [128, 2, D], BF16))
            psTR = stL0.enter_context(nc.psum_tensor("psTR", [128, 2, 128], BF16))

            sX.inc(sp.dma_start(out=rbbc[:, :], in_=rb_bc), 16)
            v.wait_ge(sX.h, sX.n)
            apply_done = []
            for tt in range(NT):
                if tt >= 2:
                    sp.wait_ge(sS.h, apply_done[tt - 2])
                sX.inc(sp.dma_start(out=xa[:, tt % 2, :],
                                    in_=x_own[tt * 128:(tt + 1) * 128, :]), 16)
                s.wait_ge(sX.h, sX.n)
                v.wait_ge(sX.h, sX.n)
                rdy = emit_ln(xa[:, tt % 2, :], xnt[:, tt % 2, :], tt % 2, epsc[:, 0:1])
                apply_done.append(rdy)
                t.wait_ge(sS.h, rdy)
                for ft in range(NF):
                    sT.inc(t.transpose(psTR[:, ft % 2, :],
                                       xnt[:, tt % 2, ft * 128:(ft + 1) * 128],
                                       identb[:, :]))
                    v.wait_ge(sT.h, sT.n)
                    sV.inc(v.tensor_copy(
                        xn_sb[:, ft, 1 + tt * 128:1 + (tt + 1) * 128],
                        psTR[:, ft % 2, :]))
                    t.wait_ge(sV.h, sV.n)
            bar()

            stB = contextlib.ExitStack()
            with stB:
                wqkv_sb = stB.enter_context(
                    nc.sbuf_tensor("wqkv_sb", [128, NF, 3 * D], BF16))
                xn_tmp = stB.enter_context(
                    nc.sbuf_tensor("xn_tmp", [128, NF, 512], BF16))
                qk_tmp = stB.enter_context(
                    nc.sbuf_tensor("qk_tmp", [128, 2, NF, 512], BF16))
                v_tmp = stB.enter_context(
                    nc.sbuf_tensor("v_tmp", [128, 4, 16, 65], BF16))
                psP = stB.enter_context(nc.psum_tensor("psP", [128, 4, 512], F32))

                sL.inc(g.dma_start(
                    out=wqkv_sb[:, :, :],
                    in_=w_qkv.rearrange("(a p) f -> p a f", p=128)), 16)
                wq_ld = sL.n
                v.memset(v_tmp[:, :, :, 64:65], 1.0)
                sV.inc(v.nop())
                t.wait_ge(sL.h, wq_ld)

                ps_free = {}
                for tc in range(8):
                    own = tc < 4
                    if own:
                        xsrc = xn_sb[:, :, 1 + tc * 512:1 + (tc + 1) * 512]
                    else:
                        for j in range(4):
                            gt = (tc - 4) * 4 + j
                            if gt >= 2 or tc > 4:
                                sp.wait_ge(sS.h, apply_done[NT + gt - 2])
                            sX.inc(sp.dma_start(
                                out=xa[:, gt % 2, :],
                                in_=x_oth[gt * 128:(gt + 1) * 128, :]), 16)
                            s.wait_ge(sX.h, sX.n)
                            v.wait_ge(sX.h, sX.n)
                            rdy = emit_ln(xa[:, gt % 2, :], xnt[:, gt % 2, :],
                                          gt % 2, epsc[:, 0:1])
                            apply_done.append(rdy)
                            t.wait_ge(sS.h, rdy)
                            for ft in range(NF):
                                sT.inc(t.transpose(
                                    psTR[:, ft % 2, :],
                                    xnt[:, gt % 2, ft * 128:(ft + 1) * 128],
                                    identb[:, :]))
                                v.wait_ge(sT.h, sT.n)
                                sV.inc(v.tensor_copy(
                                    xn_tmp[:, ft, j * 128:(j + 1) * 128],
                                    psTR[:, ft % 2, :]))
                                t.wait_ge(sV.h, sV.n)
                        xsrc = xn_tmp[:, :, :]
                        t.wait_ge(sV.h, sV.n)

                    for qk in range(2 if own else 1):
                        woff = D if qk == 0 else 0
                        dst = qk_tmp[:, qk, :, :]
                        for ft in range(NF):
                            slot = ft % 4
                            if slot in ps_free:
                                t.wait_ge(sV.h, ps_free[slot])
                            for kc in range(NF):
                                mm = t.matmul(
                                    psP[:, slot, :],
                                    wqkv_sb[:, kc,
                                            woff + ft * 128:woff + (ft + 1) * 128],
                                    xsrc[:, kc, :],
                                    start=(kc == 0), stop=(kc == NF - 1))
                            sT.inc(mm)
                            v.wait_ge(sT.h, sT.n)
                            sV.inc(v.tensor_copy(dst[:, ft, :], psP[:, slot, :]))
                            ps_free[slot] = sV.n
                        sp.wait_ge(sV.h, sV.n)
                        scr = k_scr if qk == 0 else q_scr
                        sX.inc(sp.dma_start(
                            out=scr[:, tc * 512:(tc + 1) * 512]
                            .rearrange("(a p) t -> p a t", p=128),
                            in_=dst), 16)
                    for kt in range(4):
                        for vc in range(2):
                            slot = (kt * 2 + vc) % 4
                            if slot in ps_free:
                                t.wait_ge(sV.h, ps_free[slot])
                            for kc in range(NF):
                                mm = t.matmul(
                                    psP[:, slot, :],
                                    xsrc[:, kc, kt * 128:(kt + 1) * 128],
                                    wqkv_sb[:, kc,
                                            2 * D + vc * 512:2 * D + (vc + 1) * 512],
                                    start=(kc == 0), stop=(kc == NF - 1))
                            sT.inc(mm)
                            v.wait_ge(sT.h, sT.n)
                            for h8 in range(8):
                                hg = vc * 8 + h8
                                sV.inc(v.tensor_copy(
                                    v_tmp[:, kt, hg, 0:64],
                                    psP[:, slot, h8 * 64:(h8 + 1) * 64]))
                            ps_free[slot] = sV.n
                    sp.wait_ge(sV.h, sV.n)
                    for hg in range(16):
                        sX.inc(sp.dma_start(
                            out=v_scr[hg, tc * 512:(tc + 1) * 512, :]
                            .rearrange("(a p) c -> p a c", p=128),
                            in_=v_tmp[:, :, hg, :]), 16)
                    t.wait_ge(sX.h, sX.n)
                bar()

            stR = contextlib.ExitStack()
            with stR:
                rw_sb = stR.enter_context(nc.sbuf_tensor("rw_sb", [128, NF, 16], BF16))
                lg = stR.enter_context(nc.sbuf_tensor("lg", [128, 6, 16], F32))
                rcol = stR.enter_context(nc.sbuf_tensor("rcol", [128, 4], F32))
                c0b = stR.enter_context(nc.sbuf_tensor("c0b", [128, 16], BF16))
                psR = stR.enter_context(nc.psum_tensor("psR", [128, 2, 16], F32))
                psRT = stR.enter_context(nc.psum_tensor("psRT", [16, 128], BF16))
                sL.inc(g.dma_start(out=rw_sb[:, :, :],
                                   in_=r_w.rearrange("(a p) e -> p a e", p=128)), 16)
                t.wait_ge(sL.h, sL.n)
                lg_free = {}
                for tt in range(NT):
                    slot = tt % 2
                    if slot in lg_free:
                        t.wait_ge(sV.h, lg_free[slot])
                    for kc in range(NF):
                        mm = t.matmul(psR[:, slot, :],
                                      xn_sb[:, kc, 1 + tt * 128:1 + (tt + 1) * 128],
                                      rw_sb[:, kc, :],
                                      start=(kc == 0), stop=(kc == NF - 1))
                    sT.inc(mm)
                    v.wait_ge(sT.h, sT.n)
                    sV.inc(v.tensor_tensor(lg[:, 0, :], psR[:, slot, :],
                                           rbbc[:, :], ALU.add))
                    lg_free[slot] = sV.n
                    sV.inc(v.tensor_reduce(rcol[:, 0:1], lg[:, 0, :], AX.X,
                                           ALU.max, negate=True))
                    s.wait_ge(sV.h, sV.n)
                    sS.inc(s.activation(lg[:, 1, :], lg[:, 0, :], AF.Exp,
                                        bias=rcol[:, 0:1]))
                    v.wait_ge(sS.h, sS.n)
                    sV.inc(v.tensor_copy(lg[:, 2, :], lg[:, 1, :]))
                    sV.inc(v.memset(lg[:, 3, :], 0.0))
                    sV.inc(v.memset(rcol[:, 1:2], 0.0))
                    for k in range(4):
                        sV.inc(v.tensor_reduce(rcol[:, 2:3], lg[:, 2, :], AX.X,
                                               ALU.max))
                        sV.inc(v.tensor_scalar(lg[:, 4, :], lg[:, 2, :],
                                               rcol[:, 2:3], None, ALU.is_equal))
                        sV.inc(v.tensor_tensor(lg[:, 3, :], lg[:, 4, :],
                                               lg[:, 3, :], ALU.add))
                        sV.inc(v.tensor_tensor(rcol[:, 1:2], rcol[:, 2:3],
                                               rcol[:, 1:2], ALU.add))
                        if k < 3:
                            sV.inc(v.scalar_tensor_tensor(lg[:, 2, :], lg[:, 4, :],
                                                          -2.0, lg[:, 2, :],
                                                          ALU.mult, ALU.add))
                    sV.inc(v.reciprocal(rcol[:, 3:4], rcol[:, 1:2]))
                    sV.inc(v.scalar_tensor_tensor(c0[:, tt, :], lg[:, 1, :],
                                                  rcol[:, 3:4], lg[:, 3, :],
                                                  ALU.mult, ALU.mult))
                    sV.inc(v.tensor_copy(c0b[:, :], c0[:, tt, :]))
                    t.wait_ge(sV.h, sV.n)
                    sT.inc(t.transpose(psRT[:, :], c0b[:, :], identb[:, :]))
                    v.wait_ge(sT.h, sT.n)
                    sV.inc(v.tensor_copy(c0T[:, tt * 128:(tt + 1) * 128], psRT[:, :]))
                    t.wait_ge(sV.h, sV.n)
                bar()
            if "tap_c0" in taps:
                sX.inc(sp.dma_start(out=taps["tap_c0"], in_=c0[:, :, :]), 16)
                sp.wait_ge(sX.h, sX.n)
                bar()
            if "tap_xn" in taps:
                sX.inc(sp.dma_start(out=taps["tap_xn"], in_=xn_sb[:, :, :]), 16)
                sp.wait_ge(sX.h, sX.n)
                bar()

            stA0 = contextlib.ExitStack()
            with stA0:
                acc0 = stA0.enter_context(nc.sbuf_tensor("acc0", [128, NT, D], F32))

                stAT = contextlib.ExitStack()
                with stAT:
                    k_sb = stAT.enter_context(
                        nc.sbuf_tensor("k_sb", [128, 2, 4096], BF16))
                    q_sb = stAT.enter_context(
                        nc.sbuf_tensor("q_sb", [128, 2, 2048], BF16))
                    v_sb = stAT.enter_context(
                        nc.sbuf_tensor("v_sb", [128, 2, 32, 2, 65], BF16))
                    probs = stAT.enter_context(
                        nc.sbuf_tensor("probs", [128, 2, 2, 2, 512], BF16))
                    ctxs = stAT.enter_context(nc.sbuf_tensor("ctxs", [65, 512], BF16))
                    invd = stAT.enter_context(nc.sbuf_tensor("invd", [128, 1], F32))
                    psSc = stAT.enter_context(
                        nc.psum_tensor("psSc", [128, 2, 2, 512], F32))
                    psPV = stAT.enter_context(nc.psum_tensor("psPV", [65, 2, 512], F32))
                    psCT = stAT.enter_context(nc.psum_tensor("psCT", [128, 2, 66], BF16))

                    kq_ld = {}

                    def attn_load(hp):
                        sl = hp % 2
                        sX.inc(sp.dma_start(out=k_sb[:, sl, :],
                                            in_=k_scr[hp * 128:(hp + 1) * 128, :]), 16)
                        sX.inc(sp.dma_start(out=q_sb[:, sl, :],
                                            in_=q_scr[hp * 128:(hp + 1) * 128, :]), 16)
                        for gg in range(2):
                            sX.inc(sp.dma_start(
                                out=v_sb[:, sl, :, gg, :],
                                in_=v_scr[2 * hp + gg, :, :]
                                .rearrange("(a p) c -> p a c", p=128)), 16)
                        kq_ld[hp] = sX.n

                    attn_load(0)
                    attn_load(1)
                    last_exp = [0, 0]
                    last_ctx = [0, 0]
                    for hp in range(8):
                        sl = hp % 2
                        t.wait_ge(sX.h, kq_ld[hp])
                        for qg in range(4):
                            exp_done = {}
                            pv_done = [0, 0]
                            for kp in range(17):
                                if kp < 16:
                                    for hh in range(2):
                                        if kp >= 1:
                                            t.wait_ge(sS.h, exp_done[(kp - 1, hh)])
                                        elif last_exp[hh]:
                                            t.wait_ge(sS.h, last_exp[hh])
                                        for j in range(2):
                                            kc = kp * 2 + j
                                            mm = t.matmul(
                                                psSc[:, hh, j, :],
                                                k_sb[64 * hh:64 * hh + 64, sl,
                                                     kc * 128:(kc + 1) * 128],
                                                q_sb[64 * hh:64 * hh + 64, sl,
                                                     qg * 512:(qg + 1) * 512],
                                                start=True, stop=True,
                                                tile_position=(64 * hh, 0))
                                        sT.inc(mm)
                                        s.wait_ge(sT.h, sT.n)
                                        sS.inc(s.activation(
                                            probs[:, hh, kp % 2, :, :],
                                            psSc[:, hh, :, :], AF.Exp,
                                            scale=1.0 / 32.0))
                                        exp_done[(kp, hh)] = sS.n
                                if kp >= 1:
                                    kpp = kp - 1
                                    for hh in range(2):
                                        t.wait_ge(sS.h, exp_done[(kpp, hh)])
                                        if kpp == 0 and last_ctx[hh]:
                                            t.wait_ge(sV.h, last_ctx[hh])
                                        for j in range(2):
                                            kc = kpp * 2 + j
                                            mm = t.matmul(
                                                psPV[:, hh, :],
                                                v_sb[:, sl, kc, hh, :],
                                                probs[:, hh, kpp % 2, j, :],
                                                start=(kpp == 0 and j == 0),
                                                stop=(kpp == 15 and j == 1))
                                        if kpp == 15:
                                            sT.inc(mm)
                                            pv_done[hh] = sT.n
                            last_exp = [exp_done[(15, 0)], exp_done[(15, 1)]]
                            for hh in range(2):
                                hg = hp * 2 + hh
                                v.wait_ge(sT.h, pv_done[hh])
                                sV.inc(v.tensor_copy(ctxs[:, :], psPV[:, hh, :]))
                                last_ctx[hh] = sV.n
                                t.wait_ge(sV.h, sV.n)
                                for qc in range(4):
                                    sT.inc(t.transpose(
                                        psCT[:, qc % 2, 0:65],
                                        ctxs[0:65, qc * 128:(qc + 1) * 128],
                                        identb[0:65, 0:65]))
                                    v.wait_ge(sT.h, sT.n)
                                    sV.inc(v.reciprocal(invd[:, :],
                                                        psCT[:, qc % 2, 64:65]))
                                    sV.inc(v.tensor_scalar(
                                        acc0[:, qg * 4 + qc, 64 * hg:64 * hg + 64],
                                        psCT[:, qc % 2, 0:64],
                                        invd[:, :], None, ALU.mult))
                                    t.wait_ge(sV.h, sV.n)
                        sT.inc(t.nop())
                        if hp + 2 < 8:
                            sp.wait_ge(sT.h, sT.n)
                            attn_load(hp + 2)
                    bar()

                stB2 = contextlib.ExitStack()
                with stB2:
                    b2sb = stB2.enter_context(nc.sbuf_tensor("b2sb", [16, D], BF16))
                    psB2 = stB2.enter_context(nc.psum_tensor("psB2", [128, 2, 512], F32))
                    sL.inc(g.dma_start(out=b2sb[:, :], in_=b2r), 16)
                    t.wait_ge(sL.h, sL.n)
                    ev = []
                    for tt in range(NT):
                        for dc in range(2):
                            idx = tt * 2 + dc
                            if idx >= 2:
                                t.wait_ge(sV.h, ev[idx - 2])
                            sT.inc(t.matmul(psB2[:, idx % 2, :],
                                            c0T[:, tt * 128:(tt + 1) * 128],
                                            b2sb[:, dc * 512:(dc + 1) * 512],
                                            start=True, stop=True))
                            v.wait_ge(sT.h, sT.n)
                            sV.inc(v.tensor_tensor(
                                acc0[:, tt, dc * 512:(dc + 1) * 512],
                                psB2[:, idx % 2, :],
                                acc0[:, tt, dc * 512:(dc + 1) * 512], ALU.add))
                            ev.append(sV.n)
                    bar()

                emit_ffn(nc, H, (sT, sS, sV, sL), bar, xn_sb, acc0, w1, w2,
                         b1sb, c0, n_exp=16, n_hc=2)
                if "tap_acc0" in taps:
                    sX.inc(sp.dma_start(out=taps["tap_acc0"], in_=acc0[:, :, :]), 16)
                    sp.wait_ge(sX.h, sX.n)
                    bar()

                x1t = stA0.enter_context(nc.sbuf_tensor("x1t", [128, 2, D], F32))
                ln1_app = []
                ln1_spill = []
                for tt in range(NT):
                    if tt >= 2:
                        sp.wait_ge(sS.h, ln1_app[tt - 2])
                        v.wait_ge(sS.h, ln1_app[tt - 2])
                        v.wait_ge(sX.h, ln1_spill[tt - 2])
                    sX.inc(sp.dma_start(out=xa[:, tt % 2, :],
                                        in_=x_own[tt * 128:(tt + 1) * 128, :]), 16)
                    v.wait_ge(sX.h, sX.n)
                    sV.inc(v.tensor_tensor(x1t[:, tt % 2, :], xa[:, tt % 2, :],
                                           acc0[:, tt, :], ALU.add))
                    sp.wait_ge(sV.h, sV.n)
                    sX.inc(sp.dma_start(out=x1_scr[tt * 128:(tt + 1) * 128, :],
                                        in_=x1t[:, tt % 2, :]), 16)
                    ln1_spill.append(sX.n)
                    s.wait_ge(sV.h, sV.n)
                    rdy = emit_ln(x1t[:, tt % 2, :], xnt[:, tt % 2, :], tt % 2, epsc[:, 0:1])
                    ln1_app.append(rdy)
                    t.wait_ge(sS.h, rdy)
                    for ft in range(NF):
                        sT.inc(t.transpose(psTR[:, ft % 2, :],
                                           xnt[:, tt % 2, ft * 128:(ft + 1) * 128],
                                           identb[:, :]))
                        v.wait_ge(sT.h, sT.n)
                        sV.inc(v.tensor_copy(
                            xn_sb[:, ft, 1 + tt * 128:1 + (tt + 1) * 128],
                            psTR[:, ft % 2, :]))
                        t.wait_ge(sV.h, sV.n)
                bar()
                if "tap_xn1" in taps:
                    sX.inc(sp.dma_start(out=taps["tap_xn1"], in_=xn_sb[:, :, :]), 16)
                    sp.wait_ge(sX.h, sX.n)
                    bar()

        if "tap_x1" in taps:
            with nc.sbuf_tensor("xrd", [128, D], F32) as xrd:
                for tt in range(NT):
                    sX.inc(sp.dma_start(out=xrd[:, :],
                                        in_=x1_scr[tt * 128:(tt + 1) * 128, :]), 16)
                    sp.wait_ge(sX.h, sX.n)
                    sX.inc(sp.dma_start(
                        out=taps["tap_x1"][tt * 128:(tt + 1) * 128, :],
                        in_=xrd[:, :]), 16)
                    sp.wait_ge(sX.h, sX.n)
            bar()

        emit_l1(nc, H, (sX, sL, sT, sS, sV, sG, sO, sB), bar, emit_ln, dict(
            xn_sb=xn_sb, identb=identb, identf=identf, ones1f=ones1f,
            myhot=myhot, maskc=maskc, halfc=halfc, onescol=onescol, c1=c1,
            c1T=c1T, stats=stats, junk=junk, maasb=maasb, fb1sb=fb1sb,
            WgTin=WgTin, W1=W1, W2=W2, fb2r=fb2r, wrT=wrT, wkT=wkT, wvT=wvT,
            wgT=wgT, woT=woT, decay=decay_in, faaaa=faaaa_in,
            lng_bc_in=lng_bc_in, lnb_bc_in=lnb_bc_in, cc1_in=cc1_in,
            cc1_out=cc1_out, cc2_in=cc2_in, cc2_out=cc2_out, x1_scr=x1_scr,
            y_out=y_out, taps=taps, epsc=epsc), emit_ffn)

        @block.sync
        def _(e):
            sp._replay(e)

        @block.gpsimd
        def _(e):
            g._replay(e)

        @block.tensor
        def _(e):
            t._replay(e)

        @block.scalar
        def _(e):
            s._replay(e)

        @block.vector
        def _(e):
            v._replay(e)

    return nc, taps


def emit_ffn(nc, H, sems, bar, xn_sb, acc, w1d, w2d, b1sb, ccoef, n_exp, n_hc,
             tag=""):
    sp, g, t, s, v = H["sp"], H["g"], H["t"], H["s"], H["v"]
    sT, sS, sV, sL = sems
    stF = contextlib.ExitStack()
    with stF:
        w1b = stF.enter_context(nc.sbuf_tensor("w1b_f" + tag, [128, 8, 1024], BF16))
        w2b = stF.enter_context(nc.sbuf_tensor("w2b_f" + tag, [128, 8, 1024], BF16))
        hmid = stF.enter_context(nc.sbuf_tensor("hmid_f" + tag, [128, 8, TOK], BF16))
        ps1 = stF.enter_context(nc.psum_tensor("ps1_f" + tag, [128, 2, 512], F32))
        ps2 = stF.enter_context(nc.psum_tensor("ps2_f" + tag, [128, 2, 512], F32))

        units = [(e, hc) for e in range(n_exp) for hc in range(n_hc)]
        mm1_done = {}
        mm2_done = {}
        gel_done = {}
        ev2_done = {}
        w1_ld = {}
        w2_ld = {}
        for ui, (e, hc) in enumerate(units):
            if ui >= 1:
                g.wait_ge(sT.h, mm1_done[ui - 1])
            sL.inc(g.dma_start(
                out=w1b[:, :, :],
                in_=w1d[e, :, hc * 1024:(hc + 1) * 1024]
                .rearrange("(a p) h -> p a h", p=128)), 16)
            w1_ld[ui] = sL.n
            if ui >= 1:
                g.wait_ge(sT.h, mm2_done[ui - 1])
            sL.inc(g.dma_start(
                out=w2b[:, :, :],
                in_=w2d[e, hc * 1024:(hc + 1) * 1024, :]
                .rearrange("(a p) d -> p a d", p=128)), 16)
            w2_ld[ui] = sL.n

            t.wait_ge(sL.h, w1_ld[ui])
            if ui >= 1:
                t.wait_ge(sS.h, gel_done[(ui - 1, 31)])
            for ht in range(8):
                for tcc in range(4):
                    idx = ht * 4 + tcc
                    slot = idx % 2
                    if idx >= 2:
                        t.wait_ge(sS.h, gel_done[(ui, idx - 2)])
                    for kc in range(8):
                        mm = t.matmul(ps1[:, slot, :],
                                      w1b[:, kc, ht * 128:(ht + 1) * 128],
                                      xn_sb[:, kc, 1 + tcc * 512:1 + (tcc + 1) * 512],
                                      start=(kc == 0), stop=(kc == 7))
                    sT.inc(mm)
                    s.wait_ge(sT.h, sT.n)
                    sS.inc(s.activation(
                        hmid[:, ht, tcc * 512:(tcc + 1) * 512],
                        ps1[:, slot, :], AF.Gelu,
                        bias=b1sb[:, e, hc * 8 + ht:hc * 8 + ht + 1]))
                    gel_done[(ui, idx)] = sS.n
            mm1_done[ui] = sT.n
            t.wait_ge(sL.h, w2_ld[ui])
            t.wait_ge(sS.h, gel_done[(ui, 31)])
            if ui >= 1:
                t.wait_ge(sV.h, ev2_done[(ui - 1, 31)])
            for tt in range(16):
                for dc in range(2):
                    idx = tt * 2 + dc
                    slot = idx % 2
                    if idx >= 2:
                        t.wait_ge(sV.h, ev2_done[(ui, idx - 2)])
                    for kc in range(8):
                        mm = t.matmul(ps2[:, slot, :],
                                      hmid[:, kc, tt * 128:(tt + 1) * 128],
                                      w2b[:, kc, dc * 512:(dc + 1) * 512],
                                      start=(kc == 0), stop=(kc == 7))
                    sT.inc(mm)
                    v.wait_ge(sT.h, sT.n)
                    sV.inc(v.scalar_tensor_tensor(
                        acc[:, tt, dc * 512:(dc + 1) * 512],
                        ps2[:, slot, :], ccoef[:, tt, e:e + 1],
                        acc[:, tt, dc * 512:(dc + 1) * 512],
                        ALU.mult, ALU.add))
                    ev2_done[(ui, idx)] = sV.n
            mm2_done[ui] = sT.n
        bar()


def emit_l1(nc, H, sems, bar, emit_ln, T, emit_ffn):
    sp, g, t, s, v = H["sp"], H["g"], H["t"], H["s"], H["v"]
    sX, sL, sT, sS, sV, sG, sO, sB = sems
    xn_sb = T["xn_sb"]
    identb, identf = T["identb"], T["identf"]
    c1, c1T = T["c1"], T["c1T"]
    junk = T["junk"]
    taps = T["taps"]

    stR = contextlib.ExitStack()
    with stR:
        wgt = stR.enter_context(nc.sbuf_tensor("wgt1", [128, 8, 8], BF16))
        lg = stR.enter_context(nc.sbuf_tensor("lg1", [128, 4, 8], F32))
        rcol = stR.enter_context(nc.sbuf_tensor("rcol1", [128, 4], F32))
        ind = stR.enter_context(nc.sbuf_tensor("ind1", [128, 16, 2, 8], F32))
        win = stR.enter_context(nc.sbuf_tensor("win1", [128, 16, 2, 8], F32))
        mk = stR.enter_context(nc.sbuf_tensor("mk1", [128, 16, 2], F32))
        gdump = stR.enter_context(nc.sbuf_tensor("gdump", [128, 16, 8], F32))
        indT = [stR.enter_context(nc.sbuf_tensor("indT%d" % k, [8, TOK], F32))
                for k in range(2)]
        prefT = [stR.enter_context(nc.sbuf_tensor("prefT%d" % k, [8, TOK], F32))
                 for k in range(2)]
        base_sb = stR.enter_context(nc.sbuf_tensor("base_sb", [8, 2], F32))
        stage = stR.enter_context(nc.sbuf_tensor("stage", [8, 1040], F32))
        srow = stR.enter_context(nc.sbuf_tensor("srow", [1, 1040], F32))
        tb = stR.enter_context(nc.sbuf_tensor("tb1", [8, 1040], F32))
        prow = stR.enter_context(nc.sbuf_tensor("prow", [1, 1040], F32))
        c1b = stR.enter_context(nc.sbuf_tensor("c1b", [128, 8], BF16))
        psR = stR.enter_context(nc.psum_tensor("psR1", [128, 2, 8], F32))
        psC = stR.enter_context(nc.psum_tensor("psC1", [8, 2], F32))
        psF = stR.enter_context(nc.psum_tensor("psF1", [128, 128], F32))
        psBt = stR.enter_context(nc.psum_tensor("psBt1", [128, 128], BF16))
        psBC = stR.enter_context(nc.psum_tensor("psBC", [8, 3, 512], F32))
        psB1 = stR.enter_context(nc.psum_tensor("psB1", [8, 2], F32))

        sL.inc(g.dma_start(out=wgt[:, :, :],
                           in_=T["WgTin"].rearrange("(a p) e -> p a e", p=128)), 16)
        t.wait_ge(sL.h, sL.n)
        lg_free = {}
        for tt in range(16):
            slot = tt % 2
            if slot in lg_free:
                t.wait_ge(sV.h, lg_free[slot])
            for kc in range(8):
                mm = t.matmul(psR[:, slot, :],
                              xn_sb[:, kc, 1 + tt * 128:1 + (tt + 1) * 128],
                              wgt[:, kc, :], start=(kc == 0), stop=(kc == 7))
            sT.inc(mm)
            v.wait_ge(sT.h, sT.n)
            sV.inc(v.tensor_copy(lg[:, 0, :], psR[:, slot, :]))
            lg_free[slot] = sV.n
            sV.inc(v.tensor_reduce(rcol[:, 0:1], lg[:, 0, :], AX.X, ALU.max,
                                   negate=True))
            s.wait_ge(sV.h, sV.n)
            sS.inc(s.activation(lg[:, 1, :], lg[:, 0, :], AF.Exp,
                                bias=rcol[:, 0:1], accum_out=rcol[:, 1:2]))
            v.wait_ge(sS.h, sS.n)
            sV.inc(v.reciprocal(rcol[:, 2:3], rcol[:, 1:2]))
            sV.inc(v.tensor_scalar(lg[:, 1, :], lg[:, 1, :], rcol[:, 2:3],
                                   None, ALU.mult))
            sV.inc(v.tensor_copy(lg[:, 2, :], lg[:, 1, :]))
            sV.inc(v.tensor_copy(gdump[:, tt, :], lg[:, 1, :]))
            for k in range(2):
                sV.inc(v.tensor_reduce(mk[:, tt, k:k + 1], lg[:, 2, :], AX.X,
                                       ALU.max))
                sV.inc(v.tensor_scalar(ind[:, tt, k, :], lg[:, 2, :],
                                       mk[:, tt, k:k + 1], None, ALU.is_equal))
                if k < 1:
                    sV.inc(v.scalar_tensor_tensor(lg[:, 2, :], ind[:, tt, k, :],
                                                  -2.0, lg[:, 2, :],
                                                  ALU.mult, ALU.add))
        bar()
        for k in range(2):
            for tt in range(16):
                mm = t.matmul(psC[:, k:k + 1], ind[:, tt, k, :],
                              T["onescol"][:, :],
                              start=(tt == 0), stop=(tt == 15))
            sT.inc(mm)
        v.wait_ge(sT.h, sT.n)
        sV.inc(v.tensor_copy(stage[:, 0:2], psC[:, :]))
        t.wait_ge(sV.h, sV.n)
        for ft in range(8):
            sT.inc(t.transpose(psBt[0:1, 0:128], xn_sb[:, ft, TOK:TOK + 1],
                               identb[:, :]))
            v.wait_ge(sT.h, sT.n)
            sV.inc(v.tensor_copy(srow[0:1, 16 + ft * 128:16 + (ft + 1) * 128],
                                 psBt[0:1, 0:128]))
            t.wait_ge(sV.h, sV.n)
        for k in range(2):
            sT.inc(t.transpose(psF[0:1, 0:8], stage[:, k:k + 1],
                               identf[0:8, 0:8]))
            v.wait_ge(sT.h, sT.n)
            sV.inc(v.tensor_copy(srow[0:1, k * 8:(k + 1) * 8], psF[0:1, 0:8]))
            t.wait_ge(sV.h, sV.n)
        for ch, w in ((0, 512), (1, 512), (2, 16)):
            sT.inc(t.matmul(psBC[:, ch, 0:w], T["myhot"][:, :],
                            srow[:, ch * 512:ch * 512 + w],
                            start=True, stop=True))
            v.wait_ge(sT.h, sT.n)
            sV.inc(v.tensor_copy(stage[:, ch * 512:ch * 512 + w],
                                 psBC[:, ch, 0:w]))
        g.wait_ge(sV.h, sV.n)
        sL.inc(g.dma_start(out=T["cc1_in"].ap()[:, :], in_=stage[:, :]), 16)
        g.wait_ge(sL.h, sL.n)
        sG.inc(g.collective_compute("AllReduce", ALU.add,
                                    replica_groups=[list(range(8))],
                                    ins=[T["cc1_in"].ap().opt()],
                                    outs=[T["cc1_out"].ap().opt()]))
        g.wait_ge(sG.h, sG.n)
        sL.inc(g.dma_start(out=tb[:, :], in_=T["cc1_out"].ap()[:, :]), 16)
        t.wait_ge(sL.h, sL.n)
        for ch, w in ((0, 512), (1, 512), (2, 16)):
            sT.inc(t.matmul(psBC[0:1, ch, 0:w], T["maskc"][:, 0:1],
                            tb[:, ch * 512:ch * 512 + w],
                            start=True, stop=True))
            v.wait_ge(sT.h, sT.n)
            sV.inc(v.tensor_copy(prow[0:1, ch * 512:ch * 512 + w],
                                 psBC[0:1, ch, 0:w]))
        t.wait_ge(sV.h, sV.n)
        for ft in range(8):
            sT.inc(t.transpose(psF[0:128, 0:1],
                               prow[0:1, 16 + ft * 128:16 + (ft + 1) * 128],
                               identf[0:1, 0:1]))
            v.wait_ge(sT.h, sT.n)
            sV.inc(v.tensor_copy(xn_sb[:, ft, 0:1], psF[0:128, 0:1]))
            t.wait_ge(sV.h, sV.n)
        sT.inc(t.matmul(psB1[:, 0:1], tb[:, 0:8], T["maskc"][:, 1:2],
                        start=True, stop=True))
        mm = t.matmul(psB1[:, 1:2], tb[:, 0:8], T["maskc"][:, 2:3],
                      start=True, stop=False)
        mm = t.matmul(psB1[:, 1:2], tb[:, 8:16], T["maskc"][:, 1:2],
                      start=False, stop=True)
        sT.inc(mm)
        v.wait_ge(sT.h, sT.n)
        sV.inc(v.tensor_copy(base_sb[:, :], psB1[:, :]))
        for k in range(2):
            for tt in range(16):
                sT.inc(t.transpose(psF[0:8, 0:128], ind[:, tt, k, :],
                                   identf[:, :]))
                v.wait_ge(sT.h, sT.n)
                sV.inc(v.tensor_copy(indT[k][:, tt * 128:(tt + 1) * 128],
                                     psF[0:8, 0:128]))
                t.wait_ge(sV.h, sV.n)
            sV.inc(v.tensor_tensor_scan(prefT[k][:, :], indT[k][:, :],
                                        indT[k][:, :], 0.0, ALU.add, ALU.bypass))
            sV.inc(v.tensor_tensor(prefT[k][:, :], prefT[k][:, :],
                                   indT[k][:, :], ALU.subtract))
            sV.inc(v.tensor_scalar(prefT[k][:, :], prefT[k][:, :],
                                   base_sb[:, k:k + 1], CAP, ALU.add, ALU.is_lt))
            t.wait_ge(sV.h, sV.n)
            for tt in range(16):
                sT.inc(t.transpose(psF[0:128, 0:8],
                                   prefT[k][:, tt * 128:(tt + 1) * 128],
                                   identf[0:8, 0:8]))
                v.wait_ge(sT.h, sT.n)
                sV.inc(v.tensor_copy(win[:, tt, k, :], psF[0:128, 0:8]))
                t.wait_ge(sV.h, sV.n)
        if "tap_mk" in taps:
            sX.inc(sp.dma_start(out=taps["tap_mk"], in_=mk[:, :, :]), 16)
            sX.inc(sp.dma_start(out=taps["tap_gates"], in_=gdump[:, :, :]), 16)
            sp.wait_ge(sX.h, sX.n)
        for tt in range(16):
            sV.inc(v.tensor_tensor(lg[:, 3, :], ind[:, tt, 0, :],
                                   win[:, tt, 0, :], ALU.mult))
            sV.inc(v.tensor_scalar(c1[:, tt, :], lg[:, 3, :], mk[:, tt, 0:1],
                                   None, ALU.mult))
            sV.inc(v.tensor_tensor(lg[:, 3, :], ind[:, tt, 1, :],
                                   win[:, tt, 1, :], ALU.mult))
            sV.inc(v.scalar_tensor_tensor(c1[:, tt, :], lg[:, 3, :],
                                          mk[:, tt, 1:2], c1[:, tt, :],
                                          ALU.mult, ALU.add))
            sV.inc(v.tensor_copy(c1b[:, :], c1[:, tt, :]))
            t.wait_ge(sV.h, sV.n)
            sT.inc(t.transpose(psBt[0:8, 0:128], c1b[:, :], identb[:, :]))
            v.wait_ge(sT.h, sT.n)
            sV.inc(v.tensor_copy(c1T[:, tt * 128:(tt + 1) * 128],
                                 psBt[0:8, 0:128]))
            t.wait_ge(sV.h, sV.n)
        bar()
    if "tap_c1" in taps:
        sX.inc(sp.dma_start(out=taps["tap_c1"], in_=c1[:, :, :]), 16)
        sp.wait_ge(sX.h, sX.n)
        bar()

    stY = contextlib.ExitStack()
    y_sb = stY.enter_context(nc.sbuf_tensor("y_sb", [128, 16, D], BF16))

    stScan = contextlib.ExitStack()
    with stScan:
        adds = stScan.enter_context(nc.sbuf_tensor("adds", [64, NCH, 16, 64], BF16))
        st_f = stScan.enter_context(nc.sbuf_tensor("st_f", [64, 16, 64], F32))
        st_b2 = stScan.enter_context(nc.sbuf_tensor("st_b2", [128, 16, 64], BF16))
        wcols = stScan.enter_context(nc.sbuf_tensor("wcols", [128, 16, 7], F32))
        wrs = stScan.enter_context(nc.sbuf_tensor("wrs", [128, 8, D], BF16))
        wks = stScan.enter_context(nc.sbuf_tensor("wks", [128, 8, D], BF16))
        wvs = stScan.enter_context(nc.sbuf_tensor("wvs", [128, 8, D], BF16))
        WmT = stScan.enter_context(nc.sbuf_tensor("WmT", [128, 2, 16, 256], BF16))
        stGen = contextlib.ExitStack()
        dcol = stGen.enter_context(nc.sbuf_tensor("dcol", [16, 4], F32, side="right"))
        drow = stGen.enter_context(nc.sbuf_tensor("drow", [1, 16], F32, side="right"))
        ebi = stGen.enter_context(nc.sbuf_tensor("ebi", [128, 2, 256], I32, side="right"))
        ebf = stGen.enter_context(nc.sbuf_tensor("ebf", [128, 2, 256], F32, side="right"))
        mge = stGen.enter_context(nc.sbuf_tensor("mge", [128, 2, 256], F32, side="right"))
        mdg = stGen.enter_context(nc.sbuf_tensor("mdg", [128, 2, 256], F32, side="right"))
        wtmp = stGen.enter_context(nc.sbuf_tensor("wtmp", [128, 2, 256], F32, side="right"))
        pci = stGen.enter_context(nc.sbuf_tensor("pci", [128, 1], I32, side="right"))
        pcol = stGen.enter_context(nc.sbuf_tensor("pcol", [128, 4], F32, side="right"))
        xmix = stScan.enter_context(nc.sbuf_tensor("xmix", [128, 8, 3, 256], BF16))
        dbuf = stScan.enter_context(nc.sbuf_tensor("dbuf", [128, 8, 256], BF16))
        r_c = stScan.enter_context(nc.sbuf_tensor("r_c", [128, 8, 256], BF16))
        k_c = stScan.enter_context(nc.sbuf_tensor("k_c", [128, 8, 256], BF16))
        v_c = stScan.enter_context(nc.sbuf_tensor("v_c", [128, 2, D], BF16))
        kt_c = stScan.enter_context(nc.sbuf_tensor("kt_c", [128, 2, 64], BF16))
        vw_c = stScan.enter_context(nc.sbuf_tensor("vw_c", [128, 2, 64], BF16))
        atm = stScan.enter_context(nc.sbuf_tensor("atm", [128, 2, 2, 256], BF16))
        sendb = stScan.enter_context(nc.sbuf_tensor("sendb", [64, 16, 64], F32))
        psPj = stScan.enter_context(nc.psum_tensor("psPj", [128, 2, 512], F32))
        psAT = stScan.enter_context(nc.psum_tensor("psAT", [128, 2, 2, 256], F32))
        psY = stScan.enter_context(nc.psum_tensor("psY", [128, 2, 2, 64], F32))
        psKT = stScan.enter_context(nc.psum_tensor("psKT", [128, 2, 2, 64], BF16))
        psSA = stScan.enter_context(nc.psum_tensor("psSA", [64, 2, 64], F32))

        for wbuf, wsrc in ((wrs, T["wrT"]), (wks, T["wkT"]), (wvs, T["wvT"])):
            sL.inc(g.dma_start(out=wbuf[:, :, :],
                               in_=wsrc.rearrange("(a p) j -> p a j", p=128)), 16)
        w_ld = sL.n
        sX.inc(sp.dma_start(out=dcol[:, 0:1], in_=T["decay"]), 16)
        sX.inc(sp.dma_start(out=dcol[:, 3:4], in_=T["faaaa"]), 16)
        s.wait_ge(sX.h, sX.n)
        sS.inc(s.activation(dcol[:, 1:2], dcol[:, 0:1], AF.Exp))
        v.wait_ge(sS.h, sS.n)
        sV.inc(v.tensor_scalar(dcol[:, 2:3], dcol[:, 1:2], -1.0, None, ALU.mult))
        t.wait_ge(sV.h, sV.n)
        for ci, src in ((0, 2), (1, 3)):
            sT.inc(t.transpose(psPj[0:1, 0, 0:16], dcol[:, src:src + 1],
                               identf[0:16, 0:16]))
            v.wait_ge(sT.h, sT.n)
            sV.inc(v.tensor_copy(drow[0:1, :], psPj[0:1, 0, 0:16]))
            t.wait_ge(sV.h, sV.n)
            sT.inc(t.matmul(psPj[:, 0, 0:16], T["ones1f"][:, :], drow[0:1, :],
                            start=True, stop=True))
            v.wait_ge(sT.h, sT.n)
            sV.inc(v.tensor_copy(wcols[:, :, ci:ci + 1]
                                 .rearrange("p h c -> p (h c)"),
                                 psPj[:, 0, 0:16]))
            t.wait_ge(sV.h, sV.n)
        s.wait_ge(sV.h, sV.n)
        sS.inc(s.activation(wcols[:, :, 2:3].rearrange("p h c -> p (h c)"),
                            wcols[:, :, 0:1].rearrange("p h c -> p (h c)"),
                            AF.Exp, scale=256.0))
        sG.inc(g.iota(pci[:, :], [[0, 1]], base=0, channel_multiplier=1))
        for jt in range(2):
            sG.inc(g.iota(ebi[:, jt, :], [[1, 256]], base=-(128 * jt) - 1,
                          channel_multiplier=-1))
        v.wait_ge(sG.h, sG.n)
        sV.inc(v.tensor_copy(pcol[:, 0:1], pci[:, :]))
        sV.inc(v.tensor_copy(ebf[:, :, :], ebi[:, :, :]))
        sV.inc(v.tensor_scalar(mge[:, :, :], ebf[:, :, :], 0.0, None, ALU.is_ge))
        sV.inc(v.tensor_scalar(mdg[:, :, :], ebf[:, :, :], -1.0, None,
                               ALU.is_equal))
        for h in range(16):
            for qh in range(2):
                sV.inc(v.tensor_scalar(pcol[:, 1:2], pcol[:, 0:1],
                                       float(128 * qh), None, ALU.add))
                sV.inc(v.tensor_tensor(pcol[:, 2:3], pcol[:, 1:2],
                                       wcols[:, h, 0:1], ALU.mult))
                s.wait_ge(sV.h, sV.n)
                sS.inc(s.activation(wcols[:, h, 3 + qh:4 + qh], pcol[:, 2:3],
                                    AF.Exp))
            for jt in range(2):
                sV.inc(v.tensor_scalar(pcol[:, 1:2], pcol[:, 0:1], -1.0, None,
                                       ALU.mult))
                sV.inc(v.tensor_scalar(pcol[:, 1:2], pcol[:, 1:2],
                                       float(255 - 128 * jt), None, ALU.add))
                sV.inc(v.tensor_tensor(pcol[:, 2:3], pcol[:, 1:2],
                                       wcols[:, h, 0:1], ALU.mult))
                s.wait_ge(sV.h, sV.n)
                sS.inc(s.activation(wcols[:, h, 5 + jt:6 + jt], pcol[:, 2:3],
                                    AF.Exp))
        s_wm = sS.n
        v.wait_ge(sS.h, s_wm)
        for h in range(16):
            for jt in range(2):
                sV.inc(v.tensor_scalar(wtmp[:, jt, :], ebf[:, jt, :],
                                       wcols[:, h, 0:1], 0.0, ALU.mult, ALU.min))
                s.wait_ge(sV.h, sV.n)
                sS.inc(s.activation(wtmp[:, jt, :], wtmp[:, jt, :], AF.Exp))
                v.wait_ge(sS.h, sS.n)
                sV.inc(v.tensor_tensor(wtmp[:, jt, :], wtmp[:, jt, :],
                                       mge[:, jt, :], ALU.mult))
                sV.inc(v.scalar_tensor_tensor(WmT[:, jt, h, :], mdg[:, jt, :],
                                              wcols[:, h, 1:2], wtmp[:, jt, :],
                                              ALU.mult, ALU.add))
        bar()
        stGen.close()

        t.wait_ge(sL.h, w_ld)
        proj_done = 0
        at_ev = {}
        for c in range(NCH):
            col_x = 1 + c * 256
            col_s = c * 256
            if proj_done:
                v.wait_ge(sT.h, proj_done)
            for mi in range(3):
                for ft in range(8):
                    sV.inc(v.tensor_tensor(dbuf[:, ft, :],
                                           xn_sb[:, ft, col_s:col_s + 256],
                                           xn_sb[:, ft, col_x:col_x + 256],
                                           ALU.subtract))
                    sV.inc(v.scalar_tensor_tensor(
                        xmix[:, ft, mi, :], dbuf[:, ft, :],
                        T["maasb"][:, ft, mi:mi + 1],
                        xn_sb[:, ft, col_x:col_x + 256], ALU.mult, ALU.add))
            t.wait_ge(sV.h, sV.n)
            for dst, wbuf, mi in ((r_c, wrs, 2), (k_c, wks, 0)):
                for jb in range(8):
                    slot = jb % 2
                    for kc in range(8):
                        mm = t.matmul(psPj[:, slot, 0:256],
                                      wbuf[:, kc, jb * 128:(jb + 1) * 128],
                                      xmix[:, kc, mi, :],
                                      start=(kc == 0), stop=(kc == 7))
                    sT.inc(mm)
                    v.wait_ge(sT.h, sT.n)
                    sV.inc(v.tensor_copy(dst[:, jb, :], psPj[:, slot, 0:256]))
                    t.wait_ge(sV.h, sV.n)
            for vt in range(2):
                for dc2 in range(2):
                    slot = dc2 % 2
                    for kc in range(8):
                        mm = t.matmul(psPj[:, slot, :],
                                      xmix[:, kc, 1, vt * 128:(vt + 1) * 128],
                                      wvs[:, kc, dc2 * 512:(dc2 + 1) * 512],
                                      start=(kc == 0), stop=(kc == 7))
                    sT.inc(mm)
                    v.wait_ge(sT.h, sT.n)
                    sV.inc(v.tensor_copy(v_c[:, vt, dc2 * 512:(dc2 + 1) * 512],
                                         psPj[:, slot, :]))
                    t.wait_ge(sV.h, sV.n)
            proj_done = sT.n
            for h in range(16):
                hp, hh = h // 2, h % 2
                bp = 64 * hh
                hsl = h % 2
                if h in at_ev:
                    t.wait_ge(sV.h, at_ev[h])
                for jt in range(2):
                    sT.inc(t.matmul(psAT[:, hsl, jt, :],
                                    k_c[bp:bp + 64, hp, jt * 128:(jt + 1) * 128],
                                    r_c[bp:bp + 64, hp, :],
                                    start=True, stop=True,
                                    tile_position=(bp, 0)))
                v.wait_ge(sT.h, sT.n)
                for jt in range(2):
                    sV.inc(v.tensor_tensor(atm[:, hsl, jt, :],
                                           psAT[:, hsl, jt, :],
                                           WmT[:, jt, h, :], ALU.mult))
                at_ev[h] = sV.n
                t.wait_ge(sV.h, at_ev[h])
                for qh in range(2):
                    for jt in range(2):
                        mm = t.matmul(psY[:, hsl, qh, :],
                                      atm[:, hsl, jt, qh * 128:(qh + 1) * 128],
                                      v_c[:, jt, 64 * h:64 * h + 64],
                                      start=(jt == 0), stop=(jt == 1))
                    sT.inc(mm)
                    v.wait_ge(sT.h, sT.n)
                    sV.inc(v.tensor_copy(y_sb[:, 2 * c + qh, 64 * h:64 * h + 64],
                                         psY[:, hsl, qh, :]))
                for jt in range(2):
                    sT.inc(t.transpose(psKT[:, hsl, jt, :],
                                       k_c[bp:bp + 64, hp, jt * 128:(jt + 1) * 128],
                                       identb[bp:bp + 64, bp:bp + 64]))
                    v.wait_ge(sT.h, sT.n)
                    sV.inc(v.tensor_copy(kt_c[:, jt, :], psKT[:, hsl, jt, :]))
                    sV.inc(v.tensor_scalar(vw_c[:, jt, :],
                                           v_c[:, jt, 64 * h:64 * h + 64],
                                           wcols[:, h, 5 + jt:6 + jt], None,
                                           ALU.mult))
                t.wait_ge(sV.h, sV.n)
                for jt in range(2):
                    mm = t.matmul(psSA[:, h % 2, :], kt_c[:, jt, :],
                                  vw_c[:, jt, :],
                                  start=(jt == 0), stop=(jt == 1))
                sT.inc(mm)
                v.wait_ge(sT.h, sT.n)
                sV.inc(v.tensor_copy(adds[:, c, h, :], psSA[:, h % 2, :]))
        bar()

        sV.inc(v.memset(st_f[:, :, :], 0.0))
        for c in range(NCH):
            for h in range(16):
                sV.inc(v.scalar_tensor_tensor(st_f[:, h, :], st_f[:, h, :],
                                              wcols[0:64, h, 2:3],
                                              adds[:, c, h, :],
                                              ALU.mult, ALU.add))
        for h in range(16):
            sV.inc(v.tensor_scalar(sendb[:, h, :], st_f[:, h, :],
                                   T["halfc"][:, 1:2], None, ALU.mult))
        g.wait_ge(sV.h, sV.n)
        sL.inc(g.dma_start(out=T["cc2_in"].ap()[:, :],
                           in_=sendb[:, :, :].rearrange("p h m -> p (h m)")), 16)
        g.wait_ge(sL.h, sL.n)
        sG.inc(g.collective_compute("AllReduce", ALU.add,
                                    replica_groups=[[0, 1], [2, 3], [4, 5], [6, 7]],
                                    ins=[T["cc2_in"].ap().opt()],
                                    outs=[T["cc2_out"].ap().opt()]))
        g.wait_ge(sG.h, sG.n)
        sL.inc(g.dma_start(out=sendb[:, :, :].rearrange("p h m -> p (h m)"),
                           in_=T["cc2_out"].ap()[:, :]), 16)
        v.wait_ge(sL.h, sL.n)
        for h in range(16):
            sV.inc(v.tensor_scalar(st_f[:, h, :], sendb[:, h, :],
                                   T["halfc"][:, 0:1], None, ALU.mult))
            sV.inc(v.tensor_copy(st_b2[0:64, h, :], st_f[:, h, :]))
        sp.wait_ge(sV.h, sV.n)
        sX.inc(sp.dma_start(out=st_b2[64:128, :, :], in_=st_b2[0:64, :, :]), 16)
        bar()

        rp_done = 0
        for c in range(NCH):
            col_x = 1 + c * 256
            col_s = c * 256
            if rp_done:
                v.wait_ge(sT.h, rp_done)
            for ft in range(8):
                sV.inc(v.tensor_tensor(dbuf[:, ft, :],
                                       xn_sb[:, ft, col_s:col_s + 256],
                                       xn_sb[:, ft, col_x:col_x + 256],
                                       ALU.subtract))
                sV.inc(v.scalar_tensor_tensor(
                    xmix[:, ft, 2, :], dbuf[:, ft, :],
                    T["maasb"][:, ft, 2:3],
                    xn_sb[:, ft, col_x:col_x + 256], ALU.mult, ALU.add))
            t.wait_ge(sV.h, sV.n)
            for jb in range(8):
                slot = jb % 2
                for kc in range(8):
                    mm = t.matmul(psPj[:, slot, 0:256],
                                  wrs[:, kc, jb * 128:(jb + 1) * 128],
                                  xmix[:, kc, 2, :],
                                  start=(kc == 0), stop=(kc == 7))
                sT.inc(mm)
                v.wait_ge(sT.h, sT.n)
                sV.inc(v.tensor_copy(r_c[:, jb, :], psPj[:, slot, 0:256]))
                t.wait_ge(sV.h, sV.n)
            rp_done = sT.n
            for h in range(16):
                hp, hh = h // 2, h % 2
                bp = 64 * hh
                for qh in range(2):
                    sT.inc(t.matmul(psY[:, h % 2, qh, :],
                                    r_c[bp:bp + 64, hp, qh * 128:(qh + 1) * 128],
                                    st_b2[bp:bp + 64, h, :],
                                    start=True, stop=True,
                                    tile_position=(bp, 0)))
                    v.wait_ge(sT.h, sT.n)
                    sV.inc(v.scalar_tensor_tensor(
                        y_sb[:, 2 * c + qh, 64 * h:64 * h + 64],
                        psY[:, h % 2, qh, :], wcols[:, h, 3 + qh:4 + qh],
                        y_sb[:, 2 * c + qh, 64 * h:64 * h + 64],
                        ALU.mult, ALU.add))
                    t.wait_ge(sV.h, sV.n)
            for h in range(16):
                sV.inc(v.scalar_tensor_tensor(st_f[:, h, :], st_f[:, h, :],
                                              wcols[0:64, h, 2:3],
                                              adds[:, c, h, :],
                                              ALU.mult, ALU.add))
                sV.inc(v.tensor_copy(st_b2[0:64, h, :], st_f[:, h, :]))
            sp.wait_ge(sV.h, sV.n)
            sX.inc(sp.dma_start(out=st_b2[64:128, :, :], in_=st_b2[0:64, :, :]), 16)
            t.wait_ge(sX.h, sX.n)
        bar()
    if "tap_y" in taps:
        sX.inc(sp.dma_start(out=taps["tap_y"], in_=y_sb[:, :, :]), 16)
        sp.wait_ge(sX.h, sX.n)
        bar()

    stA1 = contextlib.ExitStack()
    acc1 = stA1.enter_context(nc.sbuf_tensor("acc1", [128, 16, D], F32,
                                         side="right"))
    stPost = contextlib.ExitStack()
    with stPost:
        wgs = stPost.enter_context(nc.sbuf_tensor("wgs", [128, 8, D], BF16))
        wos = stPost.enter_context(nc.sbuf_tensor("wos", [128, 8, D], BF16))
        lngb = stPost.enter_context(nc.sbuf_tensor("lngb", [128, D], F32))
        lnbb = stPost.enter_context(nc.sbuf_tensor("lnbb", [128, D], F32))
        xgt = stPost.enter_context(nc.sbuf_tensor("xgt", [128, 8, 128], BF16))
        dbuf2 = stPost.enter_context(nc.sbuf_tensor("dbuf2", [128, 8, 128], BF16))
        g_t = stPost.enter_context(nc.sbuf_tensor("g_t", [128, D], BF16))
        ynorm = stPost.enter_context(nc.sbuf_tensor("ynorm", [128, D], F32))
        z_t = stPost.enter_context(nc.sbuf_tensor("z_t", [128, D], BF16))
        z_f = stPost.enter_context(nc.sbuf_tensor("z_f", [128, 8, 128], BF16))
        gst = stPost.enter_context(nc.sbuf_tensor("gst", [128, 16, 4], F32))
        psG = stPost.enter_context(nc.psum_tensor("psG2", [128, 2, 512], F32))
        psZ = stPost.enter_context(nc.psum_tensor("psZ2", [128, 2, 128], BF16))
        psO1 = stPost.enter_context(nc.psum_tensor("psO1", [128, 2, 512], F32))

        sL.inc(g.dma_start(out=wgs[:, :, :],
                           in_=T["wgT"].rearrange("(a p) j -> p a j", p=128)), 16)
        sL.inc(g.dma_start(out=wos[:, :, :],
                           in_=T["woT"].rearrange("(a p) j -> p a j", p=128)), 16)
        t.wait_ge(sL.h, sL.n)
        sX.inc(sp.dma_start(out=lngb[:, :], in_=T["lng_bc_in"]), 16)
        sX.inc(sp.dma_start(out=lnbb[:, :], in_=T["lnb_bc_in"]), 16)
        v.wait_ge(sX.h, sX.n)
        pv_t = pv_s = pv_v = 0
        for tt in range(16):
            col_x = 1 + tt * 128
            col_s = tt * 128
            if tt:
                v.wait_ge(sT.h, pv_t)
                v.wait_ge(sS.h, pv_s)
                s.wait_ge(sV.h, pv_v)
                t.wait_ge(sV.h, pv_v)
            for ft in range(8):
                sV.inc(v.tensor_tensor(dbuf2[:, ft, :],
                                       xn_sb[:, ft, col_s:col_s + 128],
                                       xn_sb[:, ft, col_x:col_x + 128],
                                       ALU.subtract))
                sV.inc(v.scalar_tensor_tensor(
                    xgt[:, ft, :], dbuf2[:, ft, :], T["maasb"][:, ft, 3:4],
                    xn_sb[:, ft, col_x:col_x + 128], ALU.mult, ALU.add))
            t.wait_ge(sV.h, sV.n)
            for dc in range(2):
                for kc in range(8):
                    mm = t.matmul(psG[:, dc % 2, :], xgt[:, kc, :],
                                  wgs[:, kc, dc * 512:(dc + 1) * 512],
                                  start=(kc == 0), stop=(kc == 7))
                sT.inc(mm)
                s.wait_ge(sT.h, sT.n)
                sS.inc(s.activation(g_t[:, dc * 512:(dc + 1) * 512],
                                    psG[:, dc % 2, :], AF.Silu))
            t.wait_ge(sS.h, sS.n)
            for h in range(16):
                sS.inc(s.activation(junk[:, 0:64], y_sb[:, tt, 64 * h:64 * h + 64],
                                    AF.Square, accum_out=gst[:, h, 1:2]))
                sV.inc(v.tensor_reduce(gst[:, h, 0:1],
                                       y_sb[:, tt, 64 * h:64 * h + 64],
                                       AX.X, ALU.add))
            v.wait_ge(sS.h, sS.n)
            sV.inc(v.tensor_scalar(gst[:, :, 0:1].rearrange("p h c -> p (h c)"),
                                   gst[:, :, 0:1].rearrange("p h c -> p (h c)"),
                                   1.0 / 64.0, None, ALU.mult))
            sV.inc(v.tensor_scalar(gst[:, :, 1:2].rearrange("p h c -> p (h c)"),
                                   gst[:, :, 1:2].rearrange("p h c -> p (h c)"),
                                   1.0 / 64.0, None, ALU.mult))
            sV.inc(v.tensor_tensor(gst[:, :, 2:3].rearrange("p h c -> p (h c)"),
                                   gst[:, :, 0:1].rearrange("p h c -> p (h c)"),
                                   gst[:, :, 0:1].rearrange("p h c -> p (h c)"),
                                   ALU.mult))
            sV.inc(v.tensor_tensor(gst[:, :, 2:3].rearrange("p h c -> p (h c)"),
                                   gst[:, :, 1:2].rearrange("p h c -> p (h c)"),
                                   gst[:, :, 2:3].rearrange("p h c -> p (h c)"),
                                   ALU.subtract))
            s.wait_ge(sV.h, sV.n)
            sS.inc(s.activation(gst[:, :, 2:3].rearrange("p h c -> p (h c)"),
                                gst[:, :, 2:3].rearrange("p h c -> p (h c)"),
                                AF.Ln, bias=T["epsc"][:, 1:2]))
            sS.inc(s.activation(gst[:, :, 2:3].rearrange("p h c -> p (h c)"),
                                gst[:, :, 2:3].rearrange("p h c -> p (h c)"),
                                AF.Exp, scale=-0.5))
            v.wait_ge(sS.h, sS.n)
            sV.inc(v.scalar_tensor_tensor(
                gst[:, :, 3:4].rearrange("p h c -> p (h c)"),
                gst[:, :, 0:1].rearrange("p h c -> p (h c)"), -1.0,
                gst[:, :, 2:3].rearrange("p h c -> p (h c)"),
                ALU.mult, ALU.mult))
            s.wait_ge(sV.h, sV.n)
            for h in range(16):
                sS.inc(s.activation(ynorm[:, 64 * h:64 * h + 64],
                                    y_sb[:, tt, 64 * h:64 * h + 64], AF.Identity,
                                    scale=gst[:, h, 2:3], bias=gst[:, h, 3:4]))
            v.wait_ge(sS.h, sS.n)
            sV.inc(v.tensor_tensor(ynorm[:, :], ynorm[:, :], lngb[:, :], ALU.mult))
            sV.inc(v.tensor_tensor(ynorm[:, :], ynorm[:, :], lnbb[:, :], ALU.add))
            sV.inc(v.tensor_tensor(z_t[:, :], ynorm[:, :], g_t[:, :], ALU.mult))
            t.wait_ge(sV.h, sV.n)
            for ft in range(8):
                sT.inc(t.transpose(psZ[:, ft % 2, :],
                                   z_t[:, ft * 128:(ft + 1) * 128], identb[:, :]))
                v.wait_ge(sT.h, sT.n)
                sV.inc(v.tensor_copy(z_f[:, ft, :], psZ[:, ft % 2, :]))
                t.wait_ge(sV.h, sV.n)
            for dc in range(2):
                for kc in range(8):
                    mm = t.matmul(psO1[:, dc % 2, :], z_f[:, kc, :],
                                  wos[:, kc, dc * 512:(dc + 1) * 512],
                                  start=(kc == 0), stop=(kc == 7))
                sT.inc(mm)
                v.wait_ge(sT.h, sT.n)
                sV.inc(v.tensor_copy(acc1[:, tt, dc * 512:(dc + 1) * 512],
                                     psO1[:, dc % 2, :]))
                t.wait_ge(sV.h, sV.n)
            pv_t, pv_s, pv_v = sT.n, sS.n, sV.n
        bar()
    stY.close()

    stB2 = contextlib.ExitStack()
    with stB2:
        fb2sb = stB2.enter_context(nc.sbuf_tensor("fb2sb", [8, D], BF16))
        psB2 = stB2.enter_context(nc.psum_tensor("psB2b", [128, 2, 512], F32))
        sL.inc(g.dma_start(out=fb2sb[:, :], in_=T["fb2r"]), 16)
        t.wait_ge(sL.h, sL.n)
        ev = []
        for tt in range(16):
            for dc in range(2):
                idx = tt * 2 + dc
                if idx >= 2:
                    t.wait_ge(sV.h, ev[idx - 2])
                sT.inc(t.matmul(psB2[:, idx % 2, :],
                                c1T[:, tt * 128:(tt + 1) * 128],
                                fb2sb[:, dc * 512:(dc + 1) * 512],
                                start=True, stop=True))
                v.wait_ge(sT.h, sT.n)
                sV.inc(v.tensor_tensor(acc1[:, tt, dc * 512:(dc + 1) * 512],
                                       psB2[:, idx % 2, :],
                                       acc1[:, tt, dc * 512:(dc + 1) * 512],
                                       ALU.add))
                ev.append(sV.n)
        bar()

    emit_ffn(nc, H, (sT, sS, sV, sL), bar, xn_sb, acc1, T["W1"], T["W2"],
             T["fb1sb"], c1, n_exp=8, n_hc=4, tag="1")
    if "tap_acc1" in taps:
        sX.inc(sp.dma_start(out=taps["tap_acc1"], in_=acc1[:, :, :]), 16)
        sp.wait_ge(sX.h, sX.n)
        bar()

    stFin = contextlib.ExitStack()
    with stFin:
        xf = stFin.enter_context(nc.sbuf_tensor("xf", [128, 2, D], F32))
        of = stFin.enter_context(nc.sbuf_tensor("of", [128, 2, D], F32))
        for tt in range(16):
            if tt >= 2:
                sp.wait_ge(sO.h, 16 * (tt - 1))
            sX.inc(sp.dma_start(out=xf[:, tt % 2, :],
                                in_=T["x1_scr"][tt * 128:(tt + 1) * 128, :]), 16)
            v.wait_ge(sX.h, sX.n)
            sV.inc(v.tensor_tensor(of[:, tt % 2, :], xf[:, tt % 2, :],
                                   acc1[:, tt, :], ALU.add))
            sp.wait_ge(sV.h, sV.n)
            sO.inc(sp.dma_start(out=T["y_out"][tt * 128:(tt + 1) * 128, :],
                                in_=of[:, tt % 2, :]), 16)
        sp.wait_ge(sO.h, sO.n)
    stA1.close()


def _prep_inputs(inputs):
    f32 = np.float32
    base = {}
    base["w_qkv"] = np.ascontiguousarray(inputs["L0_attn_w_qkv"], f32)
    base["r_w"] = np.ascontiguousarray(inputs["L0_ffn_router_w"], f32)
    base["rb_bc"] = np.ascontiguousarray(
        np.broadcast_to(np.asarray(inputs["L0_ffn_router_b"], f32)[None, :],
                        (128, 16)))
    base["w1"] = np.ascontiguousarray(inputs["L0_ffn_w1"], f32)
    base["b1c"] = np.ascontiguousarray(
        np.asarray(inputs["L0_ffn_b1"], f32).reshape(16, 16, 128)
        .transpose(2, 0, 1))
    base["w2"] = np.ascontiguousarray(inputs["L0_ffn_w2"], f32)
    base["b2r"] = np.ascontiguousarray(inputs["L0_ffn_b2"], f32)
    for nm in "rkvgo":
        base["w%sT" % nm] = np.ascontiguousarray(
            np.asarray(inputs["L1_attn_w_%s" % nm], f32).T)
    maa = np.stack([np.asarray(inputs["L1_attn_time_maa_%s" % nm], f32)
                    for nm in "kvrg"], axis=-1)
    base["maac"] = np.ascontiguousarray(maa.reshape(8, 128, 4).transpose(1, 0, 2))
    base["decay"] = np.ascontiguousarray(
        np.asarray(inputs["L1_attn_time_decay"], f32)[:, None])
    base["faaaa"] = np.ascontiguousarray(
        np.asarray(inputs["L1_attn_time_faaaa"], f32)[:, None])
    base["lng_bc"] = np.ascontiguousarray(
        np.broadcast_to(np.asarray(inputs["L1_attn_ln_g"], f32)[None, :],
                        (128, 1024)))
    base["lnb_bc"] = np.ascontiguousarray(
        np.broadcast_to(np.asarray(inputs["L1_attn_ln_b"], f32)[None, :],
                        (128, 1024)))
    base["WgT"] = np.ascontiguousarray(np.asarray(inputs["L1_ffn_Wg"], f32).T)
    base["W1"] = np.ascontiguousarray(inputs["L1_ffn_W1"], f32)
    base["fb1c"] = np.ascontiguousarray(
        np.asarray(inputs["L1_ffn_b1"], f32).reshape(8, 32, 128)
        .transpose(2, 0, 1))
    base["W2"] = np.ascontiguousarray(inputs["L1_ffn_W2"], f32)
    base["fb2r"] = np.ascontiguousarray(inputs["L1_ffn_b2"], f32)
    base["identb"] = np.eye(128, dtype=ml_dtypes.bfloat16)
    base["identf"] = np.eye(128, dtype=f32)
    base["ones1f"] = np.ones((1, 128), f32)
    epsc = np.zeros((128, 2), f32)
    epsc[:, 0] = EPS0
    epsc[:, 1] = EPS1
    base["epsc"] = epsc

    x = np.asarray(inputs["L0_attn_embeddings"], f32)
    in_maps = []
    for c in range(8):
        b, h = c // 2, c % 2
        m = dict(base)
        m["x_own"] = np.ascontiguousarray(x[b, h * 2048:(h + 1) * 2048, :])
        m["x_oth"] = np.ascontiguousarray(x[b, (1 - h) * 2048:(2 - h) * 2048, :])
        myhot = np.zeros((1, 8), f32)
        myhot[0, c] = 1.0
        m["myhot"] = myhot
        maskc = np.zeros((8, 3), f32)
        if h == 1:
            maskc[c - 1, 0] = 1.0
        maskc[:c, 1] = 1.0
        maskc[:, 2] = 1.0
        m["maskc"] = maskc
        halfc = np.zeros((64, 2), f32)
        halfc[:, 0] = float(h)
        halfc[:, 1] = float(1 - h)
        m["halfc"] = halfc
        in_maps.append(m)
    return in_maps


def kernel(**inputs):
    nc, _ = build()
    in_maps = _prep_inputs(inputs)
    res = run_bass_kernel_spmd(nc, in_maps, core_ids=list(range(8)))
    out = np.empty((4, 4096, 1024), np.float32)
    for c in range(8):
        b, h = c // 2, c % 2
        out[b, h * 2048:(h + 1) * 2048, :] = res.results[c]["y_own"]
    return out
